# version 15
# baseline (speedup 1.0000x reference)
import math
from contextlib import ExitStack

import numpy as np

import concourse.bass as bass
import concourse.mybir as mybir
from concourse.bass_utils import run_bass_kernel_spmd
from concourse.bass_core_rust import IndirectOffsetOnAxis

F32 = mybir.dt.float32
BF16 = mybir.dt.bfloat16
U8 = mybir.dt.uint8
I32 = mybir.dt.int32
AF = mybir.ActivationFunctionType
ALU = mybir.AluOpType
AX = mybir.AxisListType

S = 2048
D = 1024
H = 16
E = 8
FF = 4096
NTT = 16

LAM = 1.0507009873554805
ALPHA = 1.6732632423543772
LA = LAM * ALPHA
LN_LA = math.log(LA)
SSC = S ** -0.5
QSC = 64 ** -0.5
EPS = 1e-5
CAP = 512

ENGSEM = {"tensor": "T", "vector": "V", "scalar": "S"}


class Emitter:
    def __init__(self, nc, es):
        self.nc = nc
        self.ops = {"tensor": [], "vector": [], "scalar": [], "gpsimd": [], "sync": []}
        self.sems = {}
        self.cnt = {}
        self.wm = {e: {} for e in self.ops}
        self.lw = {}
        self.rd = {}
        self.pend = {}
        self._es = es

    def sem(self, name):
        if name not in self.sems:
            self.sems[name] = self._es.enter_context(self.nc.semaphore(name))
            self.cnt[name] = 0
        return self.sems[name]

    def alias(self, new, olds):
        m = self.pend.setdefault(new, {})
        for o in olds:
            ev = self.lw.pop(o, None)
            if ev:
                m[ev[0]] = max(m.get(ev[0], 0), ev[1])
            for s, v in self.rd.pop(o, {}).items():
                m[s] = max(m.get(s, 0), v)
            p = self.pend.pop(o, None)
            if p:
                for s, v in p.items():
                    m[s] = max(m.get(s, 0), v)

    def snapshot(self, olds):
        m = {}
        for o in olds:
            ev = self.lw.get(o)
            if ev:
                m[ev[0]] = max(m.get(ev[0], 0), ev[1])
            for s, v in self.rd.get(o, {}).items():
                m[s] = max(m.get(s, 0), v)
            for s, v in self.pend.get(o, {}).items():
                m[s] = max(m.get(s, 0), v)
        return m

    def seed(self, new, m):
        d = self.pend.setdefault(new, {})
        for s, v in m.items():
            d[s] = max(d.get(s, 0), v)

    def emit(self, eng, fn, reads=(), writes=(), sem=None, inc=1, pre=0):
        waits = {}

        def add_ev(ev):
            if ev:
                s, v = ev
                waits[s] = max(waits.get(s, 0), v)

        def add_map(m):
            for s, v in m.items():
                waits[s] = max(waits.get(s, 0), v)

        for r in reads:
            add_ev(self.lw.get(r))
            add_map(self.pend.pop(r, {}))
        for w in writes:
            add_ev(self.lw.get(w))
            add_map(self.rd.get(w, {}))
            add_map(self.pend.pop(w, {}))
        wl = []
        for s, v in sorted(waits.items()):
            if self.wm[eng].get(s, 0) < v:
                wl.append((self.sems[s], v))
                self.wm[eng][s] = v
        semname = sem or ENGSEM[eng]
        self.sem(semname)
        self.cnt[semname] += inc + pre
        ev = (semname, self.cnt[semname])
        semh = self.sems[semname]

        def clo(e, fn=fn, wl=wl, semh=semh, inc=inc):
            for sh, v in wl:
                e.wait_ge(sh, v)
            inst = fn(e)
            inst.then_inc(semh, inc)

        self.ops[eng].append(clo)
        for w in writes:
            self.lw[w] = ev
            self.rd[w] = {}
        for r in reads:
            d = self.rd.setdefault(r, {})
            d[ev[0]] = max(d.get(ev[0], 0), ev[1])
        return ev

    def raw(self, eng, fn):
        self.ops[eng].append(fn)


def build_nc():
    nc = bass.Bass("TRN2", target_bir_lowering=False)
    x_d = nc.dram_tensor("x", [S, D], F32, kind="ExternalInput").ap()
    wqk_d = nc.dram_tensor("wqk", [2 * D, D], F32, kind="ExternalInput").ap()
    bqk_d = nc.dram_tensor("bqk", [16, 128], F32, kind="ExternalInput").ap()
    wr_d = nc.dram_tensor("wr", [D, E], F32, kind="ExternalInput").ap()
    br_d = nc.dram_tensor("br", [1, E], F32, kind="ExternalInput").ap()
    lng_d = nc.dram_tensor("lng", [1, D], F32, kind="ExternalInput").ap()
    lnb_d = nc.dram_tensor("lnb", [1, D], F32, kind="ExternalInput").ap()
    w1_d = nc.dram_tensor("w1", [E, D, FF], F32, kind="ExternalInput").ap()
    b1_d = nc.dram_tensor("b1", [E, FF], F32, kind="ExternalInput").ap()
    w2_d = nc.dram_tensor("w2", [E, FF, D], F32, kind="ExternalInput").ap()
    b2_d = nc.dram_tensor("b2", [E, D], F32, kind="ExternalInput").ap()
    idf_d = nc.dram_tensor("idf", [128, 128], F32, kind="ExternalInput").ap()
    cst_d = nc.dram_tensor("cst", [E, 1], F32, kind="ExternalInput").ap()
    out_d = nc.dram_tensor("out", [S, D], F32, kind="ExternalOutput").ap()
    xnh_d = nc.dram_tensor("xnh", [S, D], BF16, kind="Internal").ap()
    xgh_d = nc.dram_tensor("xgh", [E * CAP, D], BF16, kind="Internal").ap()
    ysl_d = nc.dram_tensor("ysl", [E * CAP, D], BF16, kind="Internal").ap()

    es = ExitStack()
    with es:
        def sb(name, shape, dt):
            return es.enter_context(nc.sbuf_tensor(name, shape, dt))

        R1 = sb("R1", [128, 32768], U8)
        R2 = sb("R2", [128, 32768], U8)
        R3 = sb("R3", [128, 32768], U8)
        R4 = sb("R4", [128, 24576], U8)
        R5 = sb("R5", [128, 16384], U8)
        avtm = sb("avtm", [128, 16384], BF16)[:, :]

        def carve(R, byte_off, dt, p0, p1, nelem):
            return R[p0:p1, byte_off:byte_off + nelem * mybir.dt.size(dt)].bitcast(dt)

        xn_bf = R1[:, :].bitcast(BF16)
        w1v = [carve(R1, 0, BF16, 0, 128, 8192), carve(R1, 16384, BF16, 0, 128, 8192)]
        lo_fm = R2[:, :].bitcast(BF16)
        wqkT = R2[:, :].bitcast(BF16)
        xg_fm = carve(R2, 0, BF16, 0, 128, 4096)
        hm = carve(R2, 8192, BF16, 0, 128, 4096)
        ysb = carve(R2, 16384, BF16, 0, 128, 4096)
        xgt = [carve(R2, 24576, BF16, 0, 128, 1024), carve(R2, 26624, BF16, 0, 128, 1024)]
        hi_fm = R3[:, :].bitcast(BF16)
        w2v = [carve(R3, 0, BF16, 0, 128, 8192), carve(R3, 16384, BF16, 0, 128, 8192)]
        qbuf = carve(R4, 0, BF16, 0, 64, 2048)
        kbuf = carve(R4, 4096, BF16, 0, 64, 2048)
        e1b = [carve(R4, 8192, BF16, 0, 128, 1024), carve(R4, 10240, BF16, 0, 128, 1024)]
        t2b = [carve(R4, 12288, BF16, 0, 128, 1024), carve(R4, 14336, BF16, 0, 128, 1024)]
        attnT = [carve(R4, 16384 + 2048 * i, BF16, 0, 128, 1024) for i in range(4)]
        g_row = carve(R4, 0, F32, 0, 1, 1024)
        b_row = carve(R4, 4096, F32, 0, 1, 1024)
        w2buf1 = carve(R4, 0, BF16, 0, 128, 8192)
        fin2b = carve(R4, 16384, F32, 0, 128, 1024)
        xt = [carve(R5, 0, F32, 0, 128, 1024), carve(R5, 4096, F32, 0, 128, 1024)]
        xnscr = carve(R5, 8192, F32, 0, 128, 1024)
        xn2scr = carve(R5, 12288, F32, 0, 128, 1024)
        wqs = [(carve(R5, 0, F32, 0, 64, 1024), carve(R5, 4096, F32, 0, 64, 1024)),
               (carve(R5, 8192, F32, 0, 64, 1024), carve(R5, 12288, F32, 0, 64, 1024))]

        idf = sb("idfs", [128, 128], F32)[:, :]
        avfm = carve(R5, 0, BF16, 0, 64, 2048)
        vsum = sb("vsums", [128, 16], F32)[:, :]
        idbf = sb("idbf", [128, 128], BF16)[:, :]
        ohfm = sb("ohfm", [8, 2048], F32)[:, :]
        cumt = sb("cumt", [8, 2048], F32)[:, :]
        post = sb("post", [128, 16], F32)[:, :]
        posi = sb("posi", [128, 16], I32)[:, :]
        csts = sb("csts", [8, 1], F32)[:, :]
        b1T = sb("b1T", [128, 256], F32)[:, :]
        onehot = sb("onehots", [128, 128], F32)[:, :]
        gates = sb("gatess", [128, 16], F32)[:, :]
        stat = sb("stat", [128, 64], F32)[:, :]
        p8s = sb("p8s", [128, 8], F32)[:, :]
        wrst = sb("wrst", [128, 64], F32)[:, :]
        wrhi = sb("wrhi", [128, 64], BF16)[:, :]
        wrlo = sb("wrlo", [128, 64], BF16)[:, :]
        brs = sb("brs", [1, 8], F32)[:, :]
        brhi = sb("brhi", [1, 8], BF16)[:, :]
        brlo = sb("brlo", [1, 8], BF16)[:, :]
        bqst = sb("bqst", [16, 128], F32)[:, :]
        bqq = sb("bqq", [128, 16], F32)[:, :]
        b1st = sb("b1st", [8, 256], F32)[:, :]
        gbfm = sb("gbfm", [128, 16], F32)[:, :]
        ones1 = sb("ones1", [1, 128], BF16)[:, :]
        ones128 = sb("ones128", [128, 128], BF16)[:, :]
        cvec = sb("cvec", [128, 2], F32)[:, :]
        b2row = sb("b2row", [1, 1024], BF16)[:, :]

        psA = es.enter_context(nc.psum_tensor("psA", [128, 2048], F32))[:, :]
        psB = es.enter_context(nc.psum_tensor("psB", [128, 2048], F32))[:, :]

        em = Emitter(nc, es)
        for s_ in ["T", "V", "S", "dx0", "dx1", "dw1", "dw1b", "dw2a", "dw2b",
                   "dsa", "dsb", "dsc", "dsd", "dmi", "db2", "douta", "doutb",
                   "dna", "dnb", "dga", "dgb", "dia", "dib", "dys", "dya", "dyb"]:
            em.sem(s_)

        em.emit("vector", lambda e: e.memset(cvec[:, 0:1], EPS), writes=["cvec"])
        em.emit("vector", lambda e: e.memset(cvec[:, 1:2], LN_LA), writes=["cvec"])
        em.emit("vector", lambda e: e.memset(ones1, 1.0), writes=["ones1"])
        em.emit("vector", lambda e: e.memset(ones128, 1.0), writes=["ones128"])
        em.emit("sync", lambda e: e.dma_start(out=idf, in_=idf_d), writes=["idf"], sem="dmi", inc=16)
        em.emit("sync", lambda e: e.dma_start(out=bqst, in_=bqk_d), writes=["bqst"], sem="dmi", inc=16)
        em.emit("sync", lambda e: e.dma_start(
            out=wrst[:, :].rearrange("p (c e) -> p c e", c=8),
            in_=wr_d.rearrange("(c p) e -> p c e", p=128)), writes=["wrst"], sem="dmi", inc=16)
        em.emit("sync", lambda e: e.dma_start(out=brs, in_=br_d), writes=["brs"], sem="dmi", inc=16)
        em.emit("sync", lambda e: e.dma_start(out=g_row, in_=lng_d), writes=["g_row"], sem="dmi", inc=16)
        em.emit("sync", lambda e: e.dma_start(out=csts, in_=cst_d), writes=["csts"], sem="dmi", inc=16)
        em.emit("sync", lambda e: e.dma_start(out=b_row, in_=lnb_d), writes=["b_row"], sem="dmi", inc=16)
        em.emit("vector", lambda e: e.memset(attnT[2], 0.0), writes=["attnT:2"])
        for zt in range(E * CAP // 128):
            em.emit("sync", lambda e, zt=zt: e.dma_start(out=xgh_d[zt * 128:(zt + 1) * 128, :], in_=attnT[2]),
                    reads=["attnT:2"], writes=["xgh"], sem="dsc", inc=16)
        for r in ["idf", "bqst", "wrst", "brs", "g_row", "b_row", "csts"]:
            em.lw[r] = ("dmi", em.cnt["dmi"])
        em.emit("vector", lambda e: e.tensor_copy(wrhi, wrst), reads=["wrst"], writes=["wrhi"])
        em.emit("vector", lambda e: e.tensor_sub(wrlo, wrst, wrhi), reads=["wrst", "wrhi"], writes=["wrlo"])
        em.emit("vector", lambda e: e.tensor_copy(brhi, brs), reads=["brs"], writes=["brhi"])
        em.emit("vector", lambda e: e.tensor_sub(brlo, brs, brhi), reads=["brs", "brhi"], writes=["brlo"])
        em.emit("tensor", lambda e: e.transpose(psA[0:64, 0:16], bqst[0:16, 0:128:2], idf[0:16, 0:16]),
                reads=["bqst", "idf"], writes=["psA:0"])
        em.emit("vector", lambda e: e.tensor_copy(bqq[0:64, :], psA[0:64, 0:16]), reads=["psA:0"], writes=["bqq"])
        em.emit("vector", lambda e: e.tensor_copy(idbf, idf), reads=["idf"], writes=["idbf"])
        for c in range(8):
            em.emit("tensor", lambda e, c=c: e.transpose(psA[:, 512:513], g_row[0:1, c * 128:(c + 1) * 128], idf[0:1, 0:1]),
                    reads=["g_row", "idf"], writes=["psA:1"])
            em.emit("vector", lambda e, c=c: e.tensor_copy(gbfm[:, c:c + 1], psA[:, 512:513]), reads=["psA:1"], writes=["gbfm"])
            em.emit("tensor", lambda e, c=c: e.transpose(psA[:, 512:513], b_row[0:1, c * 128:(c + 1) * 128], idf[0:1, 0:1]),
                    reads=["b_row", "idf"], writes=["psA:1"])
            em.emit("vector", lambda e, c=c: e.tensor_copy(gbfm[:, 8 + c:9 + c], psA[:, 512:513]), reads=["psA:1"], writes=["gbfm"])
        for ft in range(32):
            sl = ft % 2
            bres = "b1st:%d" % sl
            bsem = "dsa" if sl == 0 else "dsb"
            em.emit("sync", lambda e, ft=ft, sl=sl: e.dma_start(out=b1st[0:8, sl * 128:(sl + 1) * 128], in_=b1_d[:, ft * 128:(ft + 1) * 128]),
                    writes=[bres], sem=bsem, inc=16)
            em.emit("tensor", lambda e, sl=sl: e.transpose(psA[:, 1024:1032], b1st[0:8, sl * 128:(sl + 1) * 128], idf[0:8, 0:8]),
                    reads=[bres, "idf"], writes=["psA:2"])
            em.emit("vector", lambda e, ft=ft: e.tensor_copy(b1T[:, ft * 8:(ft + 1) * 8], psA[:, 1024:1032]),
                    reads=["psA:2"], writes=["b1T"])


        def emit_router(tt):
            rres = "B0"

            def rtmm(e, tt=tt):
                first = True
                for c in range(8):
                    hi_sl = hi_fm[:, c * 2048 + tt * 128: c * 2048 + (tt + 1) * 128]
                    lo_sl = lo_fm[:, c * 2048 + tt * 128: c * 2048 + (tt + 1) * 128]
                    wh = wrhi[:, c * 8:(c + 1) * 8]
                    wlo = wrlo[:, c * 8:(c + 1) * 8]
                    for lhsT, rhs in ((hi_sl, wh), (hi_sl, wlo), (lo_sl, wh)):
                        e.matmul(psB[:, tt * 8:(tt + 1) * 8], lhsT, rhs, start=first, stop=False)
                        first = False
                e.matmul(psB[:, tt * 8:(tt + 1) * 8], ones1, brhi, start=False, stop=False)
                return e.matmul(psB[:, tt * 8:(tt + 1) * 8], ones1, brlo, start=False, stop=True)
            em.emit("tensor", rtmm, reads=["hi_fm", "lo_fm", "wrhi", "wrlo", "ones1", "brhi", "brlo"], writes=[rres])
            em.emit("vector", lambda e, tt=tt: e.reduce_max(stat[:, 49:50], psB[:, tt * 8:(tt + 1) * 8], axis=AX.X),
                    reads=[rres], writes=["stat3"])
            em.emit("vector", lambda e: e.tensor_scalar(stat[:, 50:51], stat[:, 49:50], -1.0, None, ALU.mult),
                    reads=["stat3"], writes=["stat3"])
            em.emit("scalar", lambda e, tt=tt: e.activation(p8s, psB[:, tt * 8:(tt + 1) * 8], AF.Exp, bias=stat[:, 50:51]),
                    reads=[rres, "stat3"], writes=["p8s"])
            em.emit("vector", lambda e: e.reduce_sum(stat[:, 51:52], p8s, axis=AX.X), reads=["p8s"], writes=["stat3"])
            em.emit("vector", lambda e, tt=tt: e.reciprocal(gates[:, tt:tt + 1], stat[:, 51:52]),
                    reads=["stat3"], writes=["gates"])
            em.emit("vector", lambda e, tt=tt: e.tensor_scalar(onehot[:, tt * 8:(tt + 1) * 8], psB[:, tt * 8:(tt + 1) * 8], stat[:, 49:50], None, ALU.is_equal),
                    reads=[rres, "stat3"], writes=["onehot"])
            q = tt % 4
            pres = "psA:%d" % q
            em.emit("tensor", lambda e, tt=tt, q=q: e.transpose(psA[0:8, q * 512:q * 512 + 128], onehot[:, tt * 8:(tt + 1) * 8], idf),
                    reads=["onehot", "idf"], writes=[pres])
            em.emit("vector", lambda e, tt=tt, q=q: e.tensor_copy(ohfm[0:8, tt * 128:(tt + 1) * 128], psA[0:8, q * 512:q * 512 + 128]),
                    reads=[pres], writes=["ohfm"])

        nscr = [xnscr, xn2scr]
        for tt in range(NTT):
            p = tt % 2
            xres = "x:%d" % p
            nres = "xn:%d" % p
            so = p * 24
            sA, sB = "sta:%d" % p, "stb:%d" % p
            em.emit("sync", lambda e, tt=tt, p=p: e.dma_start(out=xt[p], in_=x_d[tt * 128:(tt + 1) * 128, :]),
                    writes=[xres], sem="dx%d" % p, inc=16)

            def ln_chain(src, srcres, dst, dstres, so=so, sA=sA, sB=sB):
                em.emit("vector", lambda e: e.bn_stats(stat[:, so:so + 6], src[:, 0:512]), reads=[srcres], writes=[sA])
                em.emit("vector", lambda e: e.bn_stats(stat[:, so + 6:so + 12], src[:, 512:1024]), reads=[srcres], writes=[sA])
                em.emit("vector", lambda e: e.bn_aggr(stat[:, so + 12:so + 14], stat[:, so:so + 12]), reads=[sA], writes=[sA])
                em.emit("scalar", lambda e: e.activation(stat[:, so + 14:so + 15], stat[:, so + 13:so + 14], AF.Sqrt, bias=cvec[:, 0:1]),
                        reads=[sA, "cvec"], writes=[sB])
                em.emit("vector", lambda e: e.reciprocal(stat[:, so + 15:so + 16], stat[:, so + 14:so + 15]), reads=[sB], writes=[sB])
                em.emit("vector", lambda e: e.tensor_scalar(dst, src, stat[:, so + 12:so + 13], stat[:, so + 15:so + 16], ALU.subtract, ALU.mult),
                        reads=[srcres, sA, sB], writes=[dstres])

            ln_chain(xt[p], xres, nscr[p], nres)
            em.emit("vector", lambda e, tt=tt, p=p: e.tensor_copy(xn_bf[:, tt * 1024:(tt + 1) * 1024], nscr[p]),
                    reads=[nres], writes=["xn_bf"])
            ln_chain(nscr[p], nres, xt[p], xres)
            em.emit("vector", lambda e, p=p: e.tensor_copy(attnT[p], xt[p]),
                    reads=[xres], writes=["attnT:%d" % p])
            em.emit("sync", lambda e, tt=tt, p=p: e.dma_start(out=xnh_d[tt * 128:(tt + 1) * 128, :], in_=attnT[p]),
                    reads=["attnT:%d" % p], writes=["xnh"], sem="dna" if p == 0 else "dnb", inc=16)
            for c in range(8):
                q = c % 4
                pres = "psA:%d" % q
                em.emit("tensor", lambda e, c=c, q=q, p=p: e.transpose(psA[:, q * 512:q * 512 + 128], nscr[p][:, c * 128:(c + 1) * 128], idf),
                        reads=[nres, "idf"], writes=[pres])
                em.emit("vector", lambda e, c=c, q=q, tt=tt: e.tensor_copy(hi_fm[:, c * 2048 + tt * 128: c * 2048 + (tt + 1) * 128], psA[:, q * 512:q * 512 + 128]),
                        reads=[pres], writes=["hi_fm"])
                em.emit("vector", lambda e, c=c, q=q, tt=tt: e.tensor_sub(lo_fm[:, c * 2048 + tt * 128: c * 2048 + (tt + 1) * 128], psA[:, q * 512:q * 512 + 128], hi_fm[:, c * 2048 + tt * 128: c * 2048 + (tt + 1) * 128]),
                        reads=[pres, "hi_fm"], writes=["lo_fm"])
            emit_router(tt)

        em.emit("vector", lambda e: e.tensor_tensor_scan(cumt[0:8, :], ohfm[0:8, :], ohfm[0:8, :], 0.0, ALU.add, ALU.bypass),
                reads=["ohfm"], writes=["cumt"])
        em.emit("vector", lambda e: e.tensor_scalar(cumt[0:8, :], cumt[0:8, :], float(CAP), csts[0:8, 0:1], ALU.min, ALU.add),
                reads=["cumt", "csts"], writes=["cumt"])
        em.emit("vector", lambda e: e.tensor_mul(cumt[0:8, :], cumt[0:8, :], ohfm[0:8, :]),
                reads=["cumt", "ohfm"], writes=["cumt"])
        for tt in range(NTT):
            q = tt % 4
            pres = "psA:%d" % q
            em.emit("tensor", lambda e, tt=tt, q=q: e.transpose(psA[:, q * 512:q * 512 + 8], cumt[0:8, tt * 128:(tt + 1) * 128], idf[0:8, 0:8]),
                    reads=["cumt", "idf"], writes=[pres])
            em.emit("vector", lambda e, tt=tt, q=q: e.reduce_sum(post[:, tt:tt + 1], psA[:, q * 512:q * 512 + 8], axis=AX.X),
                    reads=[pres], writes=["post"])
        em.emit("vector", lambda e: e.tensor_copy(posi, post), reads=["post"], writes=["posi"])
        _loM = em.snapshot(["lo_fm"])
        em.seed("xld:0", _loM)
        em.seed("xld:1", _loM)
        for tt in range(NTT):
            sl = tt % 2
            xres = "xld:%d" % sl
            em.emit("sync", lambda e, tt=tt, sl=sl: e.dma_start(out=xgt[sl], in_=xnh_d[tt * 128:(tt + 1) * 128, :]),
                    reads=["xnh"], writes=[xres], sem="dga" if sl == 0 else "dgb", inc=16)
            em.emit("gpsimd", lambda e, tt=tt, sl=sl: e.indirect_dma_start(
                xgh_d, IndirectOffsetOnAxis(posi[:, tt:tt + 1], 0), xgt[sl], None),
                reads=[xres, "posi"], writes=["xgh"], sem="dia" if sl == 0 else "dib", inc=16)

        em.alias("wqkT", ["lo_fm", "xld:0", "xld:1"])
        em.seed("wqkT", _loM)
        em.alias("wqs:0q", ["x:0"])
        em.alias("wqs:0k", ["x:1"])
        em.alias("wqs:1q", ["xn:0"])
        em.alias("wqs:1k", ["xn:1"])
        WQSEM = {"wqs:0q": "dx0", "wqs:0k": "dx1", "wqs:1q": "dsc", "wqs:1k": "dsd"}
        for h in range(H):
            sl = h % 2
            rq, rk = "wqs:%dq" % sl, "wqs:%dk" % sl
            em.emit("sync", lambda e, h=h, sl=sl: e.dma_start(out=wqs[sl][0], in_=wqk_d[h * 128:(h + 1) * 128:2, :]),
                    writes=[rq], sem=WQSEM[rq], inc=16)
            em.emit("sync", lambda e, h=h, sl=sl: e.dma_start(out=wqs[sl][1], in_=wqk_d[h * 128 + 1:(h + 1) * 128:2, :]),
                    writes=[rk], sem=WQSEM[rk], inc=16)
            for c2 in range(2):
                src = wqs[sl][c2]
                rr = rq if c2 == 0 else rk
                for c in range(8):
                    q = c % 4
                    pres = "psA:%d" % q
                    em.emit("tensor", lambda e, src=src, c=c, q=q: e.transpose(psA[:, q * 512:q * 512 + 64], src[0:64, c * 128:(c + 1) * 128], idf[0:64, 0:64]),
                            reads=[rr, "idf"], writes=[pres])
                    off = ((h * 2 + c2) * 8 + c) * 64
                    em.emit("vector", lambda e, off=off, q=q: e.tensor_copy(wqkT[:, off:off + 64], psA[:, q * 512:q * 512 + 64]),
                            reads=[pres], writes=["wqkT"])

        em.alias("avfm", ["wqs:0q"])
        for c in range(8):
            em.emit("vector", lambda e, c=c: e.reduce_sum(vsum[:, c:c + 1], hi_fm[:, c * 2048:(c + 1) * 2048], axis=AX.X),
                    reads=["hi_fm"], writes=["vsum"])
        em.emit("vector", lambda e: e.tensor_scalar(vsum[:, 8:16], vsum[:, 0:8], LA, None, ALU.mult),
                reads=["vsum"], writes=["vsum"])

        em.alias("qbuf", ["g_row"])
        em.alias("kbuf", ["b_row"])
        for h in range(H):
            for n in range(4):
                def qmm(e, h=h, n=n):
                    for c in range(8):
                        lt = wqkT[:, ((h * 2 + 0) * 8 + c) * 64: ((h * 2 + 0) * 8 + c + 1) * 64]
                        inst = e.matmul(psA[0:64, n * 512:(n + 1) * 512], lt,
                                        hi_fm[:, c * 2048 + n * 512: c * 2048 + (n + 1) * 512],
                                        start=(c == 0), stop=(c == 7))
                    return inst
                em.emit("tensor", qmm, reads=["wqkT", "hi_fm"], writes=["psA:%d" % n])

                def kmm(e, h=h, n=n):
                    for c in range(8):
                        lt = wqkT[:, ((h * 2 + 1) * 8 + c) * 64: ((h * 2 + 1) * 8 + c + 1) * 64]
                        inst = e.matmul(psB[0:64, n * 512:(n + 1) * 512], lt,
                                        hi_fm[:, c * 2048 + n * 512: c * 2048 + (n + 1) * 512],
                                        start=(c == 0), stop=(c == 7))
                    return inst
                em.emit("tensor", kmm, reads=["wqkT", "hi_fm"], writes=["B%d" % n])
            for half in range(2):
                em.emit("vector", lambda e, h=h, half=half: e.tensor_scalar(qbuf[0:64, half * 1024:(half + 1) * 1024], psA[0:64, half * 1024:(half + 1) * 1024], bqq[0:64, h:h + 1], QSC, ALU.add, ALU.mult),
                        reads=["psA:%d" % (half * 2), "psA:%d" % (half * 2 + 1), "bqq"], writes=["qbuf"])
            em.emit("vector", lambda e: e.reduce_sum(stat[0:64, 48:49], psB[0:64, 0:2048], axis=AX.X),
                    reads=["B0", "B1", "B2", "B3"], writes=["km"])
            em.emit("vector", lambda e: e.tensor_scalar(stat[0:64, 48:49], stat[0:64, 48:49], -1.0 / S, None, ALU.mult),
                    reads=["km"], writes=["km"])
            for half in range(2):
                em.emit("vector", lambda e, half=half: e.tensor_scalar(kbuf[0:64, half * 1024:(half + 1) * 1024], psB[0:64, half * 1024:(half + 1) * 1024], stat[0:64, 48:49], None, ALU.add),
                        reads=["B%d" % (2 * half), "B%d" % (2 * half + 1), "km"], writes=["kbuf"])
            SLOT = [("psA:0", "psA:1", psA, 0), ("psA:2", "psA:3", psA, 1024), ("B2", "B3", psB, 1024)]
            for ih in range(2):
                LAG = 3

                def emit_av(jj, h=h, ih=ih):
                    def avmm(e, h=h, ih=ih, jj=jj):
                        for n2 in range(2):
                            inst = e.matmul(psB[0:64, n2 * 512:(n2 + 1) * 512],
                                            xn_bf[:, jj * 1024 + h * 64: jj * 1024 + h * 64 + 64],
                                            attnT[jj % 4][:, n2 * 512:(n2 + 1) * 512],
                                            start=(jj == 0), stop=(jj == 15))
                        return inst
                    em.emit("tensor", avmm, reads=["attnT:%d" % (jj % 4), "xn_bf"],
                            writes=["B0", "B1"])

                for j in range(NTT):
                    sa, sb_, pst, off = SLOT[j % 3]

                    def simmm(e, h=h, ih=ih, j=j, pst=pst, off=off):
                        for n2 in range(2):
                            inst = e.matmul(pst[:, off + n2 * 512: off + (n2 + 1) * 512],
                                            kbuf[0:64, j * 128:(j + 1) * 128],
                                            qbuf[0:64, ih * 1024 + n2 * 512: ih * 1024 + (n2 + 1) * 512],
                                            start=True, stop=True)
                        return inst
                    em.emit("tensor", simmm, reads=["kbuf", "qbuf"], writes=[sa, sb_])
                    em.emit("scalar", lambda e, pst=pst, off=off, j=j: e.activation(e1b[j % 2], pst[:, off:off + 1024], AF.Exp, bias=cvec[:, 1:2]),
                            reads=[sa, sb_, "cvec"], writes=["e1:%d" % (j % 2)])
                    if j % 3 == 0:
                        em.emit("scalar", lambda e, pst=pst, off=off, j=j: e.activation(t2b[j % 2], pst[:, off:off + 1024], AF.Relu, scale=LAM),
                                reads=[sa, sb_], writes=["t2:%d" % (j % 2)])
                    else:
                        em.emit("vector", lambda e, pst=pst, off=off, j=j: e.tensor_scalar(t2b[j % 2], pst[:, off:off + 1024], 0.0, LAM, ALU.max, ALU.mult),
                                reads=[sa, sb_], writes=["t2:%d" % (j % 2)])
                    em.emit("vector", lambda e, j=j: e.scalar_tensor_tensor(attnT[j % 4], e1b[j % 2], LA, t2b[j % 2], ALU.min, ALU.add),
                            reads=["e1:%d" % (j % 2), "t2:%d" % (j % 2)], writes=["attnT:%d" % (j % 4)])
                    if j >= LAG:
                        emit_av(j - LAG)
                for jj in range(NTT - LAG, NTT):
                    emit_av(jj)
                vs = vsum[(h % 2) * 64:(h % 2) * 64 + 64, 8 + h // 2: 9 + h // 2]
                for n2 in range(2):
                    em.emit("vector", lambda e, ih=ih, n2=n2, vs=vs: e.tensor_scalar(
                        avfm[0:64, ih * 1024 + n2 * 512: ih * 1024 + (n2 + 1) * 512],
                        psB[0:64, n2 * 512:(n2 + 1) * 512],
                        vs, SSC, ALU.subtract, ALU.mult),
                        reads=["B0", "B1"], writes=["avfm"])
            for it in range(NTT):
                q = it % 4
                pres = "psA:%d" % q
                em.emit("tensor", lambda e, it=it, q=q: e.transpose(psA[:, q * 512:q * 512 + 32].bitcast(BF16), avfm[0:64, it * 128:(it + 1) * 128], idbf[0:64, 0:64]),
                        reads=["avfm", "idbf"], writes=[pres])
                em.emit("vector", lambda e, it=it, q=q, h=h: e.tensor_copy(avtm[:, it * 1024 + h * 64: it * 1024 + h * 64 + 64], psA[:, q * 512:q * 512 + 32].bitcast(BF16)),
                        reads=[pres], writes=["avtm"])

        _m1 = em.snapshot(["xn_bf"])
        em.seed("w1:0", _m1)
        em.seed("w1:1", _m1)
        _m2 = em.snapshot(["wqkT"])
        for _r in ("xg_fm", "hm", "ysb", "xgt:0", "xgt:1"):
            em.seed(_r, _m2)
        _m3 = em.snapshot(["hi_fm"])
        em.seed("w2:0", _m3)
        em.seed("w2:1", _m3)
        for e_i in range(E):
            for st2 in range(4):
                sl = st2 % 2
                xres = "xgt:%d" % sl
                em.emit("sync", lambda e, e_i=e_i, st2=st2, sl=sl: e.dma_start(
                    out=xgt[sl], in_=xgh_d[e_i * CAP + st2 * 128: e_i * CAP + (st2 + 1) * 128, :]),
                    reads=["xgh"], writes=[xres], sem="dga" if sl == 0 else "dgb", inc=16)
                for c in range(8):
                    q = c % 4
                    pres = "psA:%d" % q
                    em.emit("tensor", lambda e, sl=sl, c=c, q=q: e.transpose(
                        psA[:, q * 512:q * 512 + 64].bitcast(BF16), xgt[sl][:, c * 128:(c + 1) * 128], idbf),
                        reads=[xres, "idbf"], writes=[pres])
                    em.emit("vector", lambda e, c=c, q=q, st2=st2: e.tensor_scalar(
                        xg_fm[:, c * 512 + st2 * 128: c * 512 + (st2 + 1) * 128],
                        psA[:, q * 512:q * 512 + 64].bitcast(BF16), gbfm[:, c:c + 1], gbfm[:, 8 + c:9 + c], ALU.mult, ALU.add),
                        reads=[pres, "gbfm"], writes=["xg_fm"])
            for fc in range(4):
                wsl = fc % 2

                def w1dma(e, e_i=e_i, fc=fc, wsl=wsl):
                    wsem = em.sems["dw1" if wsl == 0 else "dw1b"]
                    for c in range(8):
                        inst = e.dma_start(out=w1v[wsl][:, c * 1024:(c + 1) * 1024],
                                           in_=w1_d[e_i, c * 128:(c + 1) * 128, fc * 1024:(fc + 1) * 1024])
                        if c < 7:
                            inst.then_inc(wsem, 16)
                    return inst
                em.emit("gpsimd", w1dma, writes=["w1:%d" % wsl], sem="dw1" if wsl == 0 else "dw1b", inc=16, pre=16 * 7)

                def w2dma(e, e_i=e_i, fc=fc, wsl=wsl):
                    wsem = em.sems["dw2a" if wsl == 0 else "dw2b"]
                    for f in range(8):
                        inst = e.dma_start(out=w2v[wsl][:, f * 1024:(f + 1) * 1024],
                                           in_=w2_d[e_i, (fc * 8 + f) * 128:(fc * 8 + f + 1) * 128, :])
                        if f < 7:
                            inst.then_inc(wsem, 16)
                    return inst
                em.emit("gpsimd", w2dma, writes=["w2:%d" % wsl], sem="dw2a" if wsl == 0 else "dw2b", inc=16, pre=16 * 7)
                if fc == 3:
                    em.emit("gpsimd", lambda e, e_i=e_i: e.dma_start(out=b2row, in_=b2_d[e_i:e_i + 1, :]),
                            writes=["b2row"], sem="db2", inc=16)
                for f in range(8):
                    hq = "psA:%d" % (f % 2)

                    def amm(e, f=f, wsl=wsl):
                        for c in range(8):
                            inst = e.matmul(psA[:, (f % 2) * 512:(f % 2 + 1) * 512],
                                            w1v[wsl][:, c * 1024 + f * 128: c * 1024 + (f + 1) * 128],
                                            xg_fm[:, c * 512:(c + 1) * 512],
                                            start=(c == 0), stop=(c == 7))
                        return inst
                    em.emit("tensor", amm, reads=["w1:%d" % wsl, "xg_fm"], writes=[hq])
                    bcol = (fc * 8 + f) * 8 + e_i
                    em.emit("scalar", lambda e, f=f, bcol=bcol: e.activation(
                        hm[:, f * 512:(f + 1) * 512], psA[:, (f % 2) * 512:(f % 2 + 1) * 512],
                        AF.Relu, bias=b1T[:, bcol:bcol + 1]),
                        reads=[hq, "b1T"], writes=["hm"])
                for st2 in range(4):
                    ybanks = ["B0", "B1"] if st2 % 2 == 0 else ["B2", "B3"]
                    base = (st2 % 2) * 1024

                    def bmm(e, fc=fc, st2=st2, wsl=wsl, base=base):
                        for f in range(8):
                            for n2 in range(2):
                                inst = e.matmul(psB[:, base + n2 * 512: base + (n2 + 1) * 512],
                                                hm[:, f * 512 + st2 * 128: f * 512 + (st2 + 1) * 128],
                                                w2v[wsl][:, f * 1024 + n2 * 512: f * 1024 + (n2 + 1) * 512],
                                                start=(f == 0), stop=(f == 7 and fc != 3))
                        if fc == 3:
                            for n2 in range(2):
                                inst = e.matmul(psB[:, base + n2 * 512: base + (n2 + 1) * 512],
                                                ones1, b2row[0:1, n2 * 512:(n2 + 1) * 512],
                                                start=False, stop=True)
                        return inst
                    rd = ["hm", "w2:%d" % wsl] + (["ones1", "b2row"] if fc == 3 else [])
                    em.emit("tensor", bmm, reads=rd, writes=ybanks)
                    ysl_ = ysb[:, st2 * 1024:(st2 + 1) * 1024]
                    pv = psB[:, base:base + 1024]
                    if fc == 0:
                        em.emit("vector", lambda e, pv=pv, ysl_=ysl_: e.tensor_copy(ysl_, pv),
                                reads=ybanks, writes=["ysb"])
                    else:
                        em.emit("vector", lambda e, pv=pv, ysl_=ysl_: e.scalar_tensor_tensor(ysl_, pv, 1.0, ysl_, ALU.mult, ALU.add),
                                reads=ybanks + ["ysb"], writes=["ysb"])
            for st2 in range(4):
                em.emit("sync", lambda e, e_i=e_i, st2=st2: e.dma_start(
                    out=ysl_d[e_i * CAP + st2 * 128: e_i * CAP + (st2 + 1) * 128, :],
                    in_=ysb[:, st2 * 1024:(st2 + 1) * 1024]),
                    reads=["ysb"], writes=["ysl"], sem="dys", inc=16)

        em.alias("x:0", ["avfm"])
        em.alias("x:1", ["wqs:0k"])
        em.alias("fin1", ["wqs:1q"])
        em.alias("fin2a", ["wqs:1k"])
        em.alias("fin2b", ["attnT:0", "attnT:1"])
        em.alias("ygt:0", ["e1:0"])
        em.alias("ygt:1", ["e1:1"])
        ygt = e1b
        fin2 = [xn2scr, fin2b]
        for tt in range(NTT):
            sl = tt % 2
            xres = "x:%d" % sl
            yres = "ygt:%d" % sl
            em.emit("sync", lambda e, tt=tt, sl=sl: e.dma_start(out=xt[sl], in_=x_d[tt * 128:(tt + 1) * 128, :]),
                    writes=[xres], sem="dx%d" % sl, inc=16)
            em.emit("gpsimd", lambda e, tt=tt, sl=sl: e.indirect_dma_start(
                ygt[sl], None, ysl_d, IndirectOffsetOnAxis(posi[:, tt:tt + 1], 0)),
                reads=["ysl", "posi"], writes=[yres], sem="dya" if sl == 0 else "dyb", inc=16)
            f2res = "fin2%s" % ("a" if sl == 0 else "b")
            em.emit("vector", lambda e, tt=tt, sl=sl: e.scalar_tensor_tensor(
                xnscr, ygt[sl], gates[:, tt:tt + 1],
                avtm[:, tt * 1024:(tt + 1) * 1024], ALU.mult, ALU.add),
                reads=[yres, "gates", "avtm"], writes=["fin1"])
            em.emit("vector", lambda e, tt=tt, sl=sl: e.tensor_add(fin2[sl], xnscr, xt[sl]),
                    reads=["fin1", xres], writes=[f2res])
            em.emit("sync", lambda e, tt=tt, sl=sl: e.dma_start(out=out_d[tt * 128:(tt + 1) * 128, :], in_=fin2[sl]),
                    reads=[f2res], sem="douta" if sl == 0 else "doutb", inc=16)
        nouta, noutb = em.cnt["douta"], em.cnt["doutb"]
        da_h, db_h = em.sems["douta"], em.sems["doutb"]
        em.raw("sync", lambda e: e.wait_ge(da_h, nouta))
        em.raw("sync", lambda e: e.wait_ge(db_h, noutb))

        block = es.enter_context(nc.Block())

        @block.tensor
        def _(t):
            for f in em.ops["tensor"]:
                f(t)

        @block.vector
        def _(v):
            for f in em.ops["vector"]:
                f(v)

        @block.scalar
        def _(s_):
            for f in em.ops["scalar"]:
                f(s_)

        @block.gpsimd
        def _(g):
            for f in em.ops["gpsimd"]:
                f(g)

        @block.sync
        def _(sy):
            for f in em.ops["sync"]:
                f(sy)

    return nc


def kernel(L0_attn_x, L0_attn_W_qk, L0_attn_b_qk, L0_ffn_Wr, L0_ffn_br,
           L0_ffn_ln_g, L0_ffn_ln_b, L0_ffn_W1, L0_ffn_b1, L0_ffn_W2, L0_ffn_b2):
    x = np.ascontiguousarray(np.asarray(L0_attn_x, dtype=np.float32))
    wqk = np.ascontiguousarray(np.asarray(L0_attn_W_qk, dtype=np.float32))
    bqk = np.ascontiguousarray(np.asarray(L0_attn_b_qk, dtype=np.float32)).reshape(16, 128)
    wr = np.ascontiguousarray(np.asarray(L0_ffn_Wr, dtype=np.float32))
    br = np.ascontiguousarray(np.asarray(L0_ffn_br, dtype=np.float32)).reshape(1, E)
    lng = np.ascontiguousarray(np.asarray(L0_ffn_ln_g, dtype=np.float32)).reshape(1, D)
    lnb = np.ascontiguousarray(np.asarray(L0_ffn_ln_b, dtype=np.float32)).reshape(1, D)
    w1 = np.ascontiguousarray(np.asarray(L0_ffn_W1, dtype=np.float32))
    b1 = np.ascontiguousarray(np.asarray(L0_ffn_b1, dtype=np.float32))
    w2 = np.ascontiguousarray(np.asarray(L0_ffn_W2, dtype=np.float32))
    b2 = np.ascontiguousarray(np.asarray(L0_ffn_b2, dtype=np.float32))
    idf = np.eye(128, dtype=np.float32)
    cst = (np.arange(E, dtype=np.float32) * CAP - 1.0).reshape(E, 1)

    nc = build_nc()
    shared = {"wqk": wqk, "bqk": bqk, "wr": wr, "br": br, "lng": lng,
              "lnb": lnb, "w1": w1, "b1": b1, "w2": w2, "b2": b2, "idf": idf,
              "cst": cst}
    in_maps = [dict(shared, x=np.ascontiguousarray(x[b])) for b in range(8)]
    res = run_bass_kernel_spmd(nc, in_maps, core_ids=list(range(8)))
    out = np.stack([np.asarray(res.results[b]["out"]) for b in range(8)], axis=0)
    return out.astype(np.float32)


# version 17
# speedup vs baseline: 1.0256x; 1.0256x over previous
import math
from contextlib import ExitStack

import numpy as np

import concourse.bass as bass
import concourse.mybir as mybir
from concourse.bass_utils import run_bass_kernel_spmd
from concourse.bass_core_rust import IndirectOffsetOnAxis

F32 = mybir.dt.float32
BF16 = mybir.dt.bfloat16
U8 = mybir.dt.uint8
I32 = mybir.dt.int32
AF = mybir.ActivationFunctionType
ALU = mybir.AluOpType
AX = mybir.AxisListType

S = 2048
D = 1024
H = 16
E = 8
FF = 4096
NTT = 16

LAM = 1.0507009873554805
ALPHA = 1.6732632423543772
LA = LAM * ALPHA
LN_LA = math.log(LA)
SSC = S ** -0.5
QSC = 64 ** -0.5
EPS = 1e-5
CAP = 512

ENGSEM = {"tensor": "T", "vector": "V", "scalar": "S"}


class Emitter:
    def __init__(self, nc, es):
        self.nc = nc
        self.ops = {"tensor": [], "vector": [], "scalar": [], "gpsimd": [], "sync": []}
        self.sems = {}
        self.cnt = {}
        self.wm = {e: {} for e in self.ops}
        self.lw = {}
        self.rd = {}
        self.pend = {}
        self._es = es

    def sem(self, name):
        if name not in self.sems:
            self.sems[name] = self._es.enter_context(self.nc.semaphore(name))
            self.cnt[name] = 0
        return self.sems[name]

    def alias(self, new, olds):
        m = self.pend.setdefault(new, {})
        for o in olds:
            ev = self.lw.pop(o, None)
            if ev:
                m[ev[0]] = max(m.get(ev[0], 0), ev[1])
            for s, v in self.rd.pop(o, {}).items():
                m[s] = max(m.get(s, 0), v)
            p = self.pend.pop(o, None)
            if p:
                for s, v in p.items():
                    m[s] = max(m.get(s, 0), v)

    def snapshot(self, olds):
        m = {}
        for o in olds:
            ev = self.lw.get(o)
            if ev:
                m[ev[0]] = max(m.get(ev[0], 0), ev[1])
            for s, v in self.rd.get(o, {}).items():
                m[s] = max(m.get(s, 0), v)
            for s, v in self.pend.get(o, {}).items():
                m[s] = max(m.get(s, 0), v)
        return m

    def seed(self, new, m):
        d = self.pend.setdefault(new, {})
        for s, v in m.items():
            d[s] = max(d.get(s, 0), v)

    def emit(self, eng, fn, reads=(), writes=(), sem=None, inc=1, pre=0):
        waits = {}

        def add_ev(ev):
            if ev:
                s, v = ev
                waits[s] = max(waits.get(s, 0), v)

        def add_map(m):
            for s, v in m.items():
                waits[s] = max(waits.get(s, 0), v)

        for r in reads:
            add_ev(self.lw.get(r))
            add_map(self.pend.pop(r, {}))
        for w in writes:
            add_ev(self.lw.get(w))
            add_map(self.rd.get(w, {}))
            add_map(self.pend.pop(w, {}))
        wl = []
        for s, v in sorted(waits.items()):
            if self.wm[eng].get(s, 0) < v:
                wl.append((self.sems[s], v))
                self.wm[eng][s] = v
        semname = sem or ENGSEM[eng]
        self.sem(semname)
        self.cnt[semname] += inc + pre
        ev = (semname, self.cnt[semname])
        semh = self.sems[semname]

        def clo(e, fn=fn, wl=wl, semh=semh, inc=inc):
            for sh, v in wl:
                e.wait_ge(sh, v)
            inst = fn(e)
            inst.then_inc(semh, inc)

        self.ops[eng].append(clo)
        for w in writes:
            self.lw[w] = ev
            self.rd[w] = {}
        for r in reads:
            d = self.rd.setdefault(r, {})
            d[ev[0]] = max(d.get(ev[0], 0), ev[1])
        return ev

    def raw(self, eng, fn):
        self.ops[eng].append(fn)


def build_nc():
    nc = bass.Bass("TRN2", target_bir_lowering=False)
    x_d = nc.dram_tensor("x", [S, D], F32, kind="ExternalInput").ap()
    wqk_d = nc.dram_tensor("wqk", [2 * D, D], F32, kind="ExternalInput").ap()
    bqk_d = nc.dram_tensor("bqk", [16, 128], F32, kind="ExternalInput").ap()
    wr_d = nc.dram_tensor("wr", [D, E], F32, kind="ExternalInput").ap()
    br_d = nc.dram_tensor("br", [1, E], F32, kind="ExternalInput").ap()
    lng_d = nc.dram_tensor("lng", [1, D], F32, kind="ExternalInput").ap()
    lnb_d = nc.dram_tensor("lnb", [1, D], F32, kind="ExternalInput").ap()
    w1_d = nc.dram_tensor("w1", [E, D, FF], F32, kind="ExternalInput").ap()
    b1_d = nc.dram_tensor("b1", [E, FF], F32, kind="ExternalInput").ap()
    w2_d = nc.dram_tensor("w2", [E, FF, D], F32, kind="ExternalInput").ap()
    b2_d = nc.dram_tensor("b2", [E, D], F32, kind="ExternalInput").ap()
    idf_d = nc.dram_tensor("idf", [128, 128], F32, kind="ExternalInput").ap()
    cst_d = nc.dram_tensor("cst", [E, 1], F32, kind="ExternalInput").ap()
    out_d = nc.dram_tensor("out", [S, D], F32, kind="ExternalOutput").ap()
    xnh_d = nc.dram_tensor("xnh", [S, D], BF16, kind="Internal").ap()
    xgh_d = nc.dram_tensor("xgh", [E * CAP, D], BF16, kind="Internal").ap()
    ysl_d = nc.dram_tensor("ysl", [E * CAP, D], BF16, kind="Internal").ap()

    es = ExitStack()
    with es:
        def sb(name, shape, dt):
            return es.enter_context(nc.sbuf_tensor(name, shape, dt))

        R1 = sb("R1", [128, 32768], U8)
        R2 = sb("R2", [128, 32768], U8)
        R3 = sb("R3", [128, 32768], U8)
        R4 = sb("R4", [128, 24576], U8)
        R5 = sb("R5", [128, 16384], U8)
        avtm = sb("avtm", [128, 16384], BF16)[:, :]

        def carve(R, byte_off, dt, p0, p1, nelem):
            return R[p0:p1, byte_off:byte_off + nelem * mybir.dt.size(dt)].bitcast(dt)

        xn_bf = R1[:, :].bitcast(BF16)
        w1v = [carve(R1, 0, BF16, 0, 128, 8192), carve(R1, 16384, BF16, 0, 128, 8192)]
        lo_fm = R2[:, :].bitcast(BF16)
        wqkT = R2[:, :].bitcast(BF16)
        xg_fm = carve(R2, 0, BF16, 0, 128, 4096)
        hm = carve(R2, 8192, BF16, 0, 128, 4096)
        ysb = carve(R2, 16384, BF16, 0, 128, 4096)
        xgt = [carve(R2, 24576, BF16, 0, 128, 1024), carve(R2, 26624, BF16, 0, 128, 1024)]
        hi_fm = R3[:, :].bitcast(BF16)
        w2v = [carve(R3, 0, BF16, 0, 128, 8192), carve(R3, 16384, BF16, 0, 128, 8192)]
        qbuf = carve(R4, 0, BF16, 0, 64, 2048)
        kbuf = carve(R4, 4096, BF16, 0, 64, 2048)
        e1b = [carve(R4, 8192, BF16, 0, 128, 1024), carve(R4, 10240, BF16, 0, 128, 1024)]
        t2b = [carve(R4, 12288, BF16, 0, 128, 1024), carve(R4, 14336, BF16, 0, 128, 1024)]
        attnT = [carve(R4, 16384 + 2048 * i, BF16, 0, 128, 1024) for i in range(4)]
        g_row = carve(R4, 0, F32, 0, 1, 1024)
        b_row = carve(R4, 4096, F32, 0, 1, 1024)
        w2buf1 = carve(R4, 0, BF16, 0, 128, 8192)
        fin2b = carve(R4, 16384, F32, 0, 128, 1024)
        xt = [carve(R5, 0, F32, 0, 128, 1024), carve(R5, 4096, F32, 0, 128, 1024)]
        xnscr = carve(R5, 8192, F32, 0, 128, 1024)
        xn2scr = carve(R5, 12288, F32, 0, 128, 1024)
        wqs = [(carve(R5, 0, F32, 0, 64, 1024), carve(R5, 4096, F32, 0, 64, 1024)),
               (carve(R5, 8192, F32, 0, 64, 1024), carve(R5, 12288, F32, 0, 64, 1024))]

        idf = sb("idfs", [128, 128], F32)[:, :]
        avfm = carve(R5, 0, BF16, 0, 64, 2048)
        vsum = sb("vsums", [128, 16], F32)[:, :]
        idbf = sb("idbf", [128, 128], BF16)[:, :]
        ohfm = sb("ohfm", [8, 2048], F32)[:, :]
        cumt = sb("cumt", [8, 2048], F32)[:, :]
        post = sb("post", [128, 16], F32)[:, :]
        posi = sb("posi", [128, 16], I32)[:, :]
        csts = sb("csts", [8, 1], F32)[:, :]
        b1T = sb("b1T", [128, 256], F32)[:, :]
        onehot = sb("onehots", [128, 128], F32)[:, :]
        gates = sb("gatess", [128, 16], F32)[:, :]
        stat = sb("stat", [128, 64], F32)[:, :]
        p8s = sb("p8s", [128, 8], F32)[:, :]
        wrst = sb("wrst", [128, 64], F32)[:, :]
        wrhi = sb("wrhi", [128, 64], BF16)[:, :]
        wrlo = sb("wrlo", [128, 64], BF16)[:, :]
        brs = sb("brs", [1, 8], F32)[:, :]
        brhi = sb("brhi", [1, 8], BF16)[:, :]
        brlo = sb("brlo", [1, 8], BF16)[:, :]
        bqst = sb("bqst", [16, 128], F32)[:, :]
        bqq = sb("bqq", [128, 16], F32)[:, :]
        b1st = sb("b1st", [8, 256], F32)[:, :]
        gbfm = sb("gbfm", [128, 16], F32)[:, :]
        ones1 = sb("ones1", [1, 128], BF16)[:, :]
        ones128 = sb("ones128", [128, 128], BF16)[:, :]
        cvec = sb("cvec", [128, 2], F32)[:, :]
        b2row = sb("b2row", [1, 1024], BF16)[:, :]

        psA = es.enter_context(nc.psum_tensor("psA", [128, 2048], F32))[:, :]
        psB = es.enter_context(nc.psum_tensor("psB", [128, 2048], F32))[:, :]

        em = Emitter(nc, es)
        for s_ in ["T", "V", "S", "dx0", "dx1", "dw1", "dw1b", "dw2a", "dw2b",
                   "dsa", "dsb", "dsc", "dsd", "dmi", "db2", "douta", "doutb",
                   "dna", "dnb", "dga", "dgb", "dia", "dib", "dys", "dya", "dyb", "dzr"]:
            em.sem(s_)

        em.emit("vector", lambda e: e.memset(cvec[:, 0:1], EPS), writes=["cvec"])
        em.emit("vector", lambda e: e.memset(cvec[:, 1:2], LN_LA), writes=["cvec"])
        em.emit("vector", lambda e: e.memset(ones1, 1.0), writes=["ones1"])
        em.emit("vector", lambda e: e.memset(ones128, 1.0), writes=["ones128"])
        em.emit("sync", lambda e: e.dma_start(out=idf, in_=idf_d), writes=["idf"], sem="dmi", inc=16)
        em.emit("sync", lambda e: e.dma_start(out=bqst, in_=bqk_d), writes=["bqst"], sem="dmi", inc=16)
        em.emit("sync", lambda e: e.dma_start(
            out=wrst[:, :].rearrange("p (c e) -> p c e", c=8),
            in_=wr_d.rearrange("(c p) e -> p c e", p=128)), writes=["wrst"], sem="dmi", inc=16)
        em.emit("sync", lambda e: e.dma_start(out=brs, in_=br_d), writes=["brs"], sem="dmi", inc=16)
        em.emit("sync", lambda e: e.dma_start(out=g_row, in_=lng_d), writes=["g_row"], sem="dmi", inc=16)
        em.emit("sync", lambda e: e.dma_start(out=csts, in_=cst_d), writes=["csts"], sem="dmi", inc=16)
        em.emit("sync", lambda e: e.dma_start(out=b_row, in_=lnb_d), writes=["b_row"], sem="dmi", inc=16)
        em.emit("vector", lambda e: e.memset(attnT[2], 0.0), writes=["attnT:2"])
        for zt in range(E * CAP // 128):
            em.emit("gpsimd", lambda e, zt=zt: e.dma_start(out=xgh_d[zt * 128:(zt + 1) * 128, :], in_=attnT[2]),
                    reads=["attnT:2"], writes=["xgh0:%d" % zt], sem="dzr", inc=16)
        for r in ["idf", "bqst", "wrst", "brs", "g_row", "b_row", "csts"]:
            em.lw[r] = ("dmi", em.cnt["dmi"])
        em.emit("vector", lambda e: e.tensor_copy(wrhi, wrst), reads=["wrst"], writes=["wrhi"])
        em.emit("vector", lambda e: e.tensor_sub(wrlo, wrst, wrhi), reads=["wrst", "wrhi"], writes=["wrlo"])
        em.emit("vector", lambda e: e.tensor_copy(brhi, brs), reads=["brs"], writes=["brhi"])
        em.emit("vector", lambda e: e.tensor_sub(brlo, brs, brhi), reads=["brs", "brhi"], writes=["brlo"])
        em.emit("tensor", lambda e: e.transpose(psA[0:64, 0:16], bqst[0:16, 0:128:2], idf[0:16, 0:16]),
                reads=["bqst", "idf"], writes=["psA:0"])
        em.emit("vector", lambda e: e.tensor_copy(bqq[0:64, :], psA[0:64, 0:16]), reads=["psA:0"], writes=["bqq"])
        em.emit("vector", lambda e: e.tensor_copy(idbf, idf), reads=["idf"], writes=["idbf"])
        for c in range(8):
            em.emit("tensor", lambda e, c=c: e.transpose(psA[:, 512:513], g_row[0:1, c * 128:(c + 1) * 128], idf[0:1, 0:1]),
                    reads=["g_row", "idf"], writes=["psA:1"])
            em.emit("vector", lambda e, c=c: e.tensor_copy(gbfm[:, c:c + 1], psA[:, 512:513]), reads=["psA:1"], writes=["gbfm"])
            em.emit("tensor", lambda e, c=c: e.transpose(psA[:, 512:513], b_row[0:1, c * 128:(c + 1) * 128], idf[0:1, 0:1]),
                    reads=["b_row", "idf"], writes=["psA:1"])
            em.emit("vector", lambda e, c=c: e.tensor_copy(gbfm[:, 8 + c:9 + c], psA[:, 512:513]), reads=["psA:1"], writes=["gbfm"])
        for ft in range(32):
            sl = ft % 2
            bres = "b1st:%d" % sl
            bsem = "dsa" if sl == 0 else "dsb"
            em.emit("sync", lambda e, ft=ft, sl=sl: e.dma_start(out=b1st[0:8, sl * 128:(sl + 1) * 128], in_=b1_d[:, ft * 128:(ft + 1) * 128]),
                    writes=[bres], sem=bsem, inc=16)
            em.emit("tensor", lambda e, sl=sl: e.transpose(psA[:, 1024:1032], b1st[0:8, sl * 128:(sl + 1) * 128], idf[0:8, 0:8]),
                    reads=[bres, "idf"], writes=["psA:2"])
            em.emit("vector", lambda e, ft=ft: e.tensor_copy(b1T[:, ft * 8:(ft + 1) * 8], psA[:, 1024:1032]),
                    reads=["psA:2"], writes=["b1T"])


        def emit_router(tt):
            rres = "B0"

            def rtmm(e, tt=tt):
                first = True
                for c in range(8):
                    hi_sl = hi_fm[:, c * 2048 + tt * 128: c * 2048 + (tt + 1) * 128]
                    lo_sl = lo_fm[:, c * 2048 + tt * 128: c * 2048 + (tt + 1) * 128]
                    wh = wrhi[:, c * 8:(c + 1) * 8]
                    wlo = wrlo[:, c * 8:(c + 1) * 8]
                    for lhsT, rhs in ((hi_sl, wh), (hi_sl, wlo), (lo_sl, wh)):
                        e.matmul(psB[:, tt * 8:(tt + 1) * 8], lhsT, rhs, start=first, stop=False)
                        first = False
                e.matmul(psB[:, tt * 8:(tt + 1) * 8], ones1, brhi, start=False, stop=False)
                return e.matmul(psB[:, tt * 8:(tt + 1) * 8], ones1, brlo, start=False, stop=True)
            em.emit("tensor", rtmm, reads=["hi_fm", "lo_fm", "wrhi", "wrlo", "ones1", "brhi", "brlo"], writes=[rres])
            em.emit("vector", lambda e, tt=tt: e.reduce_max(stat[:, 49:50], psB[:, tt * 8:(tt + 1) * 8], axis=AX.X),
                    reads=[rres], writes=["stat3"])
            em.emit("vector", lambda e: e.tensor_scalar(stat[:, 50:51], stat[:, 49:50], -1.0, None, ALU.mult),
                    reads=["stat3"], writes=["stat3"])
            em.emit("scalar", lambda e, tt=tt: e.activation(p8s, psB[:, tt * 8:(tt + 1) * 8], AF.Exp, bias=stat[:, 50:51]),
                    reads=[rres, "stat3"], writes=["p8s"])
            em.emit("vector", lambda e: e.reduce_sum(stat[:, 51:52], p8s, axis=AX.X), reads=["p8s"], writes=["stat3"])
            em.emit("vector", lambda e, tt=tt: e.reciprocal(gates[:, tt:tt + 1], stat[:, 51:52]),
                    reads=["stat3"], writes=["gates"])
            em.emit("vector", lambda e, tt=tt: e.tensor_scalar(onehot[:, tt * 8:(tt + 1) * 8], psB[:, tt * 8:(tt + 1) * 8], stat[:, 49:50], None, ALU.is_equal),
                    reads=[rres, "stat3"], writes=["onehot"])
            q = tt % 4
            pres = "psA:%d" % q
            em.emit("tensor", lambda e, tt=tt, q=q: e.transpose(psA[0:8, q * 512:q * 512 + 128], onehot[:, tt * 8:(tt + 1) * 8], idf),
                    reads=["onehot", "idf"], writes=[pres])
            em.emit("vector", lambda e, tt=tt, q=q: e.tensor_copy(ohfm[0:8, tt * 128:(tt + 1) * 128], psA[0:8, q * 512:q * 512 + 128]),
                    reads=[pres], writes=["ohfm"])

        nscr = [xnscr, xn2scr]
        for tt in range(NTT):
            p = tt % 2
            xres = "x:%d" % p
            nres = "xn:%d" % p
            so = p * 24
            sA, sB = "sta:%d" % p, "stb:%d" % p
            em.emit("sync", lambda e, tt=tt, p=p: e.dma_start(out=xt[p], in_=x_d[tt * 128:(tt + 1) * 128, :]),
                    writes=[xres], sem="dx%d" % p, inc=16)

            def ln_chain(src, srcres, dst, dstres, so=so, sA=sA, sB=sB):
                em.emit("vector", lambda e: e.bn_stats(stat[:, so:so + 6], src[:, 0:512]), reads=[srcres], writes=[sA])
                em.emit("vector", lambda e: e.bn_stats(stat[:, so + 6:so + 12], src[:, 512:1024]), reads=[srcres], writes=[sA])
                em.emit("vector", lambda e: e.bn_aggr(stat[:, so + 12:so + 14], stat[:, so:so + 12]), reads=[sA], writes=[sA])
                em.emit("scalar", lambda e: e.activation(stat[:, so + 14:so + 15], stat[:, so + 13:so + 14], AF.Sqrt, bias=cvec[:, 0:1]),
                        reads=[sA, "cvec"], writes=[sB])
                em.emit("vector", lambda e: e.reciprocal(stat[:, so + 15:so + 16], stat[:, so + 14:so + 15]), reads=[sB], writes=[sB])
                em.emit("vector", lambda e: e.tensor_scalar(dst, src, stat[:, so + 12:so + 13], stat[:, so + 15:so + 16], ALU.subtract, ALU.mult),
                        reads=[srcres, sA, sB], writes=[dstres])

            ln_chain(xt[p], xres, nscr[p], nres)
            em.emit("vector", lambda e, tt=tt, p=p: e.tensor_copy(xn_bf[:, tt * 1024:(tt + 1) * 1024], nscr[p]),
                    reads=[nres], writes=["xn_bf"])
            ln_chain(nscr[p], nres, xt[p], xres)
            em.emit("vector", lambda e, p=p: e.tensor_copy(attnT[p], xt[p]),
                    reads=[xres], writes=["attnT:%d" % p])
            em.emit("sync", lambda e, tt=tt, p=p: e.dma_start(out=xnh_d[tt * 128:(tt + 1) * 128, :], in_=attnT[p]),
                    reads=["attnT:%d" % p], writes=["xnh"], sem="dna" if p == 0 else "dnb", inc=16)
            for c in range(8):
                q = c % 4
                pres = "psA:%d" % q
                em.emit("tensor", lambda e, c=c, q=q, p=p: e.transpose(psA[:, q * 512:q * 512 + 128], nscr[p][:, c * 128:(c + 1) * 128], idf),
                        reads=[nres, "idf"], writes=[pres])
                em.emit("vector", lambda e, c=c, q=q, tt=tt: e.tensor_copy(hi_fm[:, c * 2048 + tt * 128: c * 2048 + (tt + 1) * 128], psA[:, q * 512:q * 512 + 128]),
                        reads=[pres], writes=["hi_fm"])
                em.emit("vector", lambda e, c=c, q=q, tt=tt: e.tensor_sub(lo_fm[:, c * 2048 + tt * 128: c * 2048 + (tt + 1) * 128], psA[:, q * 512:q * 512 + 128], hi_fm[:, c * 2048 + tt * 128: c * 2048 + (tt + 1) * 128]),
                        reads=[pres, "hi_fm"], writes=["lo_fm"])
            emit_router(tt)

        em.emit("vector", lambda e: e.tensor_tensor_scan(cumt[0:8, :], ohfm[0:8, :], ohfm[0:8, :], 0.0, ALU.add, ALU.bypass),
                reads=["ohfm"], writes=["cumt"])
        em.emit("vector", lambda e: e.tensor_scalar(cumt[0:8, :], cumt[0:8, :], float(CAP), csts[0:8, 0:1], ALU.min, ALU.add),
                reads=["cumt", "csts"], writes=["cumt"])
        em.emit("vector", lambda e: e.tensor_mul(cumt[0:8, :], cumt[0:8, :], ohfm[0:8, :]),
                reads=["cumt", "ohfm"], writes=["cumt"])
        for tt in range(NTT):
            q = tt % 4
            pres = "psA:%d" % q
            em.emit("tensor", lambda e, tt=tt, q=q: e.transpose(psA[:, q * 512:q * 512 + 8], cumt[0:8, tt * 128:(tt + 1) * 128], idf[0:8, 0:8]),
                    reads=["cumt", "idf"], writes=[pres])
            em.emit("vector", lambda e, tt=tt, q=q: e.reduce_sum(post[:, tt:tt + 1], psA[:, q * 512:q * 512 + 8], axis=AX.X),
                    reads=[pres], writes=["post"])
        em.emit("vector", lambda e: e.tensor_copy(posi, post), reads=["post"], writes=["posi"])
        _loM = em.snapshot(["lo_fm"])
        em.seed("xld:0", _loM)
        em.seed("xld:1", _loM)
        em.seed("xgh", em.snapshot(["xgh0:%d" % zt for zt in range(E * CAP // 128)]))
        for tt in range(NTT):
            sl = tt % 2
            xres = "xld:%d" % sl
            em.emit("sync", lambda e, tt=tt, sl=sl: e.dma_start(out=xgt[sl], in_=xnh_d[tt * 128:(tt + 1) * 128, :]),
                    reads=["xnh"], writes=[xres], sem="dga" if sl == 0 else "dgb", inc=16)
            em.emit("gpsimd", lambda e, tt=tt, sl=sl: e.indirect_dma_start(
                xgh_d, IndirectOffsetOnAxis(posi[:, tt:tt + 1], 0), xgt[sl], None),
                reads=[xres, "posi"], writes=["xgh"], sem="dia" if sl == 0 else "dib", inc=16)

        em.alias("wqkT", ["lo_fm", "xld:0", "xld:1"])
        em.seed("wqkT", _loM)
        em.alias("wqs:0q", ["x:0"])
        em.alias("wqs:0k", ["x:1"])
        em.alias("wqs:1q", ["xn:0"])
        em.alias("wqs:1k", ["xn:1"])
        WQSEM = {"wqs:0q": "dx0", "wqs:0k": "dx1", "wqs:1q": "dsc", "wqs:1k": "dsd"}
        for h in range(H):
            sl = h % 2
            rq, rk = "wqs:%dq" % sl, "wqs:%dk" % sl
            em.emit("sync", lambda e, h=h, sl=sl: e.dma_start(out=wqs[sl][0], in_=wqk_d[h * 128:(h + 1) * 128:2, :]),
                    writes=[rq], sem=WQSEM[rq], inc=16)
            em.emit("sync", lambda e, h=h, sl=sl: e.dma_start(out=wqs[sl][1], in_=wqk_d[h * 128 + 1:(h + 1) * 128:2, :]),
                    writes=[rk], sem=WQSEM[rk], inc=16)
            for c2 in range(2):
                src = wqs[sl][c2]
                rr = rq if c2 == 0 else rk
                for c in range(8):
                    q = c % 4
                    pres = "psA:%d" % q
                    em.emit("tensor", lambda e, src=src, c=c, q=q: e.transpose(psA[:, q * 512:q * 512 + 64], src[0:64, c * 128:(c + 1) * 128], idf[0:64, 0:64]),
                            reads=[rr, "idf"], writes=[pres])
                    off = ((h * 2 + c2) * 8 + c) * 64
                    em.emit("vector", lambda e, off=off, q=q: e.tensor_copy(wqkT[:, off:off + 64], psA[:, q * 512:q * 512 + 64]),
                            reads=[pres], writes=["wqkT"])

        em.alias("avfm", ["wqs:0q"])
        for c in range(8):
            em.emit("vector", lambda e, c=c: e.reduce_sum(vsum[:, c:c + 1], hi_fm[:, c * 2048:(c + 1) * 2048], axis=AX.X),
                    reads=["hi_fm"], writes=["vsum"])
        em.emit("vector", lambda e: e.tensor_scalar(vsum[:, 8:16], vsum[:, 0:8], LA, None, ALU.mult),
                reads=["vsum"], writes=["vsum"])

        em.alias("qbuf", ["g_row"])
        em.alias("kbuf", ["b_row"])
        for h in range(H):
            for n in range(4):
                def qmm(e, h=h, n=n):
                    for c in range(8):
                        lt = wqkT[:, ((h * 2 + 0) * 8 + c) * 64: ((h * 2 + 0) * 8 + c + 1) * 64]
                        inst = e.matmul(psA[0:64, n * 512:(n + 1) * 512], lt,
                                        hi_fm[:, c * 2048 + n * 512: c * 2048 + (n + 1) * 512],
                                        start=(c == 0), stop=(c == 7))
                    return inst
                em.emit("tensor", qmm, reads=["wqkT", "hi_fm"], writes=["psA:%d" % n])

                def kmm(e, h=h, n=n):
                    for c in range(8):
                        lt = wqkT[:, ((h * 2 + 1) * 8 + c) * 64: ((h * 2 + 1) * 8 + c + 1) * 64]
                        inst = e.matmul(psB[0:64, n * 512:(n + 1) * 512], lt,
                                        hi_fm[:, c * 2048 + n * 512: c * 2048 + (n + 1) * 512],
                                        start=(c == 0), stop=(c == 7))
                    return inst
                em.emit("tensor", kmm, reads=["wqkT", "hi_fm"], writes=["B%d" % n])
            for half in range(2):
                em.emit("vector", lambda e, h=h, half=half: e.tensor_scalar(qbuf[0:64, half * 1024:(half + 1) * 1024], psA[0:64, half * 1024:(half + 1) * 1024], bqq[0:64, h:h + 1], QSC, ALU.add, ALU.mult),
                        reads=["psA:%d" % (half * 2), "psA:%d" % (half * 2 + 1), "bqq"], writes=["qbuf"])
            em.emit("vector", lambda e: e.reduce_sum(stat[0:64, 48:49], psB[0:64, 0:2048], axis=AX.X),
                    reads=["B0", "B1", "B2", "B3"], writes=["km"])
            em.emit("vector", lambda e: e.tensor_scalar(stat[0:64, 48:49], stat[0:64, 48:49], -1.0 / S, None, ALU.mult),
                    reads=["km"], writes=["km"])
            for half in range(2):
                em.emit("vector", lambda e, half=half: e.tensor_scalar(kbuf[0:64, half * 1024:(half + 1) * 1024], psB[0:64, half * 1024:(half + 1) * 1024], stat[0:64, 48:49], None, ALU.add),
                        reads=["B%d" % (2 * half), "B%d" % (2 * half + 1), "km"], writes=["kbuf"])
            SLOT = [("psA:0", "psA:1", psA, 0), ("psA:2", "psA:3", psA, 1024), ("B2", "B3", psB, 1024)]
            for ih in range(2):
                LAG = 3

                def emit_av(jj, h=h, ih=ih):
                    def avmm(e, h=h, ih=ih, jj=jj):
                        for n2 in range(2):
                            inst = e.matmul(psB[0:64, n2 * 512:(n2 + 1) * 512],
                                            xn_bf[:, jj * 1024 + h * 64: jj * 1024 + h * 64 + 64],
                                            attnT[jj % 4][:, n2 * 512:(n2 + 1) * 512],
                                            start=(jj == 0), stop=(jj == 15))
                        return inst
                    em.emit("tensor", avmm, reads=["attnT:%d" % (jj % 4), "xn_bf"],
                            writes=["B0", "B1"])

                for j in range(NTT):
                    sa, sb_, pst, off = SLOT[j % 3]

                    def simmm(e, h=h, ih=ih, j=j, pst=pst, off=off):
                        for n2 in range(2):
                            inst = e.matmul(pst[:, off + n2 * 512: off + (n2 + 1) * 512],
                                            kbuf[0:64, j * 128:(j + 1) * 128],
                                            qbuf[0:64, ih * 1024 + n2 * 512: ih * 1024 + (n2 + 1) * 512],
                                            start=True, stop=True)
                        return inst
                    em.emit("tensor", simmm, reads=["kbuf", "qbuf"], writes=[sa, sb_])
                    em.emit("scalar", lambda e, pst=pst, off=off, j=j: e.activation(e1b[j % 2], pst[:, off:off + 1024], AF.Exp, bias=cvec[:, 1:2]),
                            reads=[sa, sb_, "cvec"], writes=["e1:%d" % (j % 2)])
                    if j % 3 == 0:
                        em.emit("scalar", lambda e, pst=pst, off=off, j=j: e.activation(t2b[j % 2], pst[:, off:off + 1024], AF.Relu, scale=LAM),
                                reads=[sa, sb_], writes=["t2:%d" % (j % 2)])
                    else:
                        em.emit("vector", lambda e, pst=pst, off=off, j=j: e.tensor_scalar(t2b[j % 2], pst[:, off:off + 1024], 0.0, LAM, ALU.max, ALU.mult),
                                reads=[sa, sb_], writes=["t2:%d" % (j % 2)])
                    em.emit("vector", lambda e, j=j: e.scalar_tensor_tensor(attnT[j % 4], e1b[j % 2], LA, t2b[j % 2], ALU.min, ALU.add),
                            reads=["e1:%d" % (j % 2), "t2:%d" % (j % 2)], writes=["attnT:%d" % (j % 4)])
                    if j >= LAG:
                        emit_av(j - LAG)
                for jj in range(NTT - LAG, NTT):
                    emit_av(jj)
                vs = vsum[(h % 2) * 64:(h % 2) * 64 + 64, 8 + h // 2: 9 + h // 2]
                for n2 in range(2):
                    em.emit("vector", lambda e, ih=ih, n2=n2, vs=vs: e.tensor_scalar(
                        avfm[0:64, ih * 1024 + n2 * 512: ih * 1024 + (n2 + 1) * 512],
                        psB[0:64, n2 * 512:(n2 + 1) * 512],
                        vs, SSC, ALU.subtract, ALU.mult),
                        reads=["B0", "B1"], writes=["avfm"])
            for it in range(NTT):
                q = it % 4
                pres = "psA:%d" % q
                em.emit("tensor", lambda e, it=it, q=q: e.transpose(psA[:, q * 512:q * 512 + 32].bitcast(BF16), avfm[0:64, it * 128:(it + 1) * 128], idbf[0:64, 0:64]),
                        reads=["avfm", "idbf"], writes=[pres])
                em.emit("vector", lambda e, it=it, q=q, h=h: e.tensor_copy(avtm[:, it * 1024 + h * 64: it * 1024 + h * 64 + 64], psA[:, q * 512:q * 512 + 32].bitcast(BF16)),
                        reads=[pres], writes=["avtm"])

        _m1 = em.snapshot(["xn_bf"])
        em.seed("w1:0", _m1)
        em.seed("w1:1", _m1)
        _m2 = em.snapshot(["wqkT"])
        for _r in ("xg_fm", "hm", "ysb", "xgt:0", "xgt:1"):
            em.seed(_r, _m2)
        _m3 = em.snapshot(["hi_fm"])
        em.seed("w2:0", _m3)
        em.seed("w2:1", _m3)
        for e_i in range(E):
            for st2 in range(4):
                sl = st2 % 2
                xres = "xgt:%d" % sl
                em.emit("sync", lambda e, e_i=e_i, st2=st2, sl=sl: e.dma_start(
                    out=xgt[sl], in_=xgh_d[e_i * CAP + st2 * 128: e_i * CAP + (st2 + 1) * 128, :]),
                    reads=["xgh"], writes=[xres], sem="dga" if sl == 0 else "dgb", inc=16)
                for c in range(8):
                    q = c % 4
                    pres = "psA:%d" % q
                    em.emit("tensor", lambda e, sl=sl, c=c, q=q: e.transpose(
                        psA[:, q * 512:q * 512 + 64].bitcast(BF16), xgt[sl][:, c * 128:(c + 1) * 128], idbf),
                        reads=[xres, "idbf"], writes=[pres])
                    em.emit("vector", lambda e, c=c, q=q, st2=st2: e.tensor_scalar(
                        xg_fm[:, c * 512 + st2 * 128: c * 512 + (st2 + 1) * 128],
                        psA[:, q * 512:q * 512 + 64].bitcast(BF16), gbfm[:, c:c + 1], gbfm[:, 8 + c:9 + c], ALU.mult, ALU.add),
                        reads=[pres, "gbfm"], writes=["xg_fm"])
            for fc in range(4):
                wsl = fc % 2

                def w1dma(e, e_i=e_i, fc=fc, wsl=wsl):
                    wsem = em.sems["dw1" if wsl == 0 else "dw1b"]
                    for c in range(8):
                        inst = e.dma_start(out=w1v[wsl][:, c * 1024:(c + 1) * 1024],
                                           in_=w1_d[e_i, c * 128:(c + 1) * 128, fc * 1024:(fc + 1) * 1024])
                        if c < 7:
                            inst.then_inc(wsem, 16)
                    return inst
                em.emit("gpsimd", w1dma, writes=["w1:%d" % wsl], sem="dw1" if wsl == 0 else "dw1b", inc=16, pre=16 * 7)

                def w2dma(e, e_i=e_i, fc=fc, wsl=wsl):
                    wsem = em.sems["dw2a" if wsl == 0 else "dw2b"]
                    for f in range(8):
                        inst = e.dma_start(out=w2v[wsl][:, f * 1024:(f + 1) * 1024],
                                           in_=w2_d[e_i, (fc * 8 + f) * 128:(fc * 8 + f + 1) * 128, :])
                        if f < 7:
                            inst.then_inc(wsem, 16)
                    return inst
                em.emit("gpsimd", w2dma, writes=["w2:%d" % wsl], sem="dw2a" if wsl == 0 else "dw2b", inc=16, pre=16 * 7)
                if fc == 3:
                    em.emit("gpsimd", lambda e, e_i=e_i: e.dma_start(out=b2row, in_=b2_d[e_i:e_i + 1, :]),
                            writes=["b2row"], sem="db2", inc=16)
                for f in range(8):
                    hq = "psA:%d" % (f % 2)

                    def amm(e, f=f, wsl=wsl):
                        for c in range(8):
                            inst = e.matmul(psA[:, (f % 2) * 512:(f % 2 + 1) * 512],
                                            w1v[wsl][:, c * 1024 + f * 128: c * 1024 + (f + 1) * 128],
                                            xg_fm[:, c * 512:(c + 1) * 512],
                                            start=(c == 0), stop=(c == 7))
                        return inst
                    em.emit("tensor", amm, reads=["w1:%d" % wsl, "xg_fm"], writes=[hq])
                    bcol = (fc * 8 + f) * 8 + e_i
                    em.emit("scalar", lambda e, f=f, bcol=bcol: e.activation(
                        hm[:, f * 512:(f + 1) * 512], psA[:, (f % 2) * 512:(f % 2 + 1) * 512],
                        AF.Relu, bias=b1T[:, bcol:bcol + 1]),
                        reads=[hq, "b1T"], writes=["hm"])
                for st2 in range(4):
                    ybanks = ["B0", "B1"] if st2 % 2 == 0 else ["B2", "B3"]
                    base = (st2 % 2) * 1024

                    def bmm(e, fc=fc, st2=st2, wsl=wsl, base=base):
                        for f in range(8):
                            for n2 in range(2):
                                inst = e.matmul(psB[:, base + n2 * 512: base + (n2 + 1) * 512],
                                                hm[:, f * 512 + st2 * 128: f * 512 + (st2 + 1) * 128],
                                                w2v[wsl][:, f * 1024 + n2 * 512: f * 1024 + (n2 + 1) * 512],
                                                start=(f == 0), stop=(f == 7 and fc != 3))
                        if fc == 3:
                            for n2 in range(2):
                                inst = e.matmul(psB[:, base + n2 * 512: base + (n2 + 1) * 512],
                                                ones1, b2row[0:1, n2 * 512:(n2 + 1) * 512],
                                                start=False, stop=True)
                        return inst
                    rd = ["hm", "w2:%d" % wsl] + (["ones1", "b2row"] if fc == 3 else [])
                    em.emit("tensor", bmm, reads=rd, writes=ybanks)
                    ysl_ = ysb[:, st2 * 1024:(st2 + 1) * 1024]
                    pv = psB[:, base:base + 1024]
                    if fc == 0:
                        em.emit("vector", lambda e, pv=pv, ysl_=ysl_: e.tensor_copy(ysl_, pv),
                                reads=ybanks, writes=["ysb"])
                    else:
                        em.emit("vector", lambda e, pv=pv, ysl_=ysl_: e.scalar_tensor_tensor(ysl_, pv, 1.0, ysl_, ALU.mult, ALU.add),
                                reads=ybanks + ["ysb"], writes=["ysb"])
            for st2 in range(4):
                em.emit("sync", lambda e, e_i=e_i, st2=st2: e.dma_start(
                    out=ysl_d[e_i * CAP + st2 * 128: e_i * CAP + (st2 + 1) * 128, :],
                    in_=ysb[:, st2 * 1024:(st2 + 1) * 1024]),
                    reads=["ysb"], writes=["ysl"], sem="dys", inc=16)

        em.alias("x:0", ["avfm"])
        em.alias("x:1", ["wqs:0k"])
        em.alias("fin1", ["wqs:1q"])
        em.alias("fin2a", ["wqs:1k"])
        em.alias("fin2b", ["attnT:0", "attnT:1"])
        em.alias("ygt:0", ["e1:0"])
        em.alias("ygt:1", ["e1:1"])
        ygt = e1b
        fin2 = [xn2scr, fin2b]
        for tt in range(NTT):
            sl = tt % 2
            xres = "x:%d" % sl
            yres = "ygt:%d" % sl
            em.emit("sync", lambda e, tt=tt, sl=sl: e.dma_start(out=xt[sl], in_=x_d[tt * 128:(tt + 1) * 128, :]),
                    writes=[xres], sem="dx%d" % sl, inc=16)
            em.emit("gpsimd", lambda e, tt=tt, sl=sl: e.indirect_dma_start(
                ygt[sl], None, ysl_d, IndirectOffsetOnAxis(posi[:, tt:tt + 1], 0)),
                reads=["ysl", "posi"], writes=[yres], sem="dya" if sl == 0 else "dyb", inc=16)
            f2res = "fin2%s" % ("a" if sl == 0 else "b")
            em.emit("vector", lambda e, tt=tt, sl=sl: e.scalar_tensor_tensor(
                xnscr, ygt[sl], gates[:, tt:tt + 1],
                avtm[:, tt * 1024:(tt + 1) * 1024], ALU.mult, ALU.add),
                reads=[yres, "gates", "avtm"], writes=["fin1"])
            em.emit("vector", lambda e, tt=tt, sl=sl: e.tensor_add(fin2[sl], xnscr, xt[sl]),
                    reads=["fin1", xres], writes=[f2res])
            em.emit("sync", lambda e, tt=tt, sl=sl: e.dma_start(out=out_d[tt * 128:(tt + 1) * 128, :], in_=fin2[sl]),
                    reads=[f2res], sem="douta" if sl == 0 else "doutb", inc=16)
        nouta, noutb = em.cnt["douta"], em.cnt["doutb"]
        da_h, db_h = em.sems["douta"], em.sems["doutb"]
        em.raw("sync", lambda e: e.wait_ge(da_h, nouta))
        em.raw("sync", lambda e: e.wait_ge(db_h, noutb))

        block = es.enter_context(nc.Block())

        @block.tensor
        def _(t):
            for f in em.ops["tensor"]:
                f(t)

        @block.vector
        def _(v):
            for f in em.ops["vector"]:
                f(v)

        @block.scalar
        def _(s_):
            for f in em.ops["scalar"]:
                f(s_)

        @block.gpsimd
        def _(g):
            for f in em.ops["gpsimd"]:
                f(g)

        @block.sync
        def _(sy):
            for f in em.ops["sync"]:
                f(sy)

    return nc


def kernel(L0_attn_x, L0_attn_W_qk, L0_attn_b_qk, L0_ffn_Wr, L0_ffn_br,
           L0_ffn_ln_g, L0_ffn_ln_b, L0_ffn_W1, L0_ffn_b1, L0_ffn_W2, L0_ffn_b2):
    x = np.ascontiguousarray(np.asarray(L0_attn_x, dtype=np.float32))
    wqk = np.ascontiguousarray(np.asarray(L0_attn_W_qk, dtype=np.float32))
    bqk = np.ascontiguousarray(np.asarray(L0_attn_b_qk, dtype=np.float32)).reshape(16, 128)
    wr = np.ascontiguousarray(np.asarray(L0_ffn_Wr, dtype=np.float32))
    br = np.ascontiguousarray(np.asarray(L0_ffn_br, dtype=np.float32)).reshape(1, E)
    lng = np.ascontiguousarray(np.asarray(L0_ffn_ln_g, dtype=np.float32)).reshape(1, D)
    lnb = np.ascontiguousarray(np.asarray(L0_ffn_ln_b, dtype=np.float32)).reshape(1, D)
    w1 = np.ascontiguousarray(np.asarray(L0_ffn_W1, dtype=np.float32))
    b1 = np.ascontiguousarray(np.asarray(L0_ffn_b1, dtype=np.float32))
    w2 = np.ascontiguousarray(np.asarray(L0_ffn_W2, dtype=np.float32))
    b2 = np.ascontiguousarray(np.asarray(L0_ffn_b2, dtype=np.float32))
    idf = np.eye(128, dtype=np.float32)
    cst = (np.arange(E, dtype=np.float32) * CAP - 1.0).reshape(E, 1)

    nc = build_nc()
    shared = {"wqk": wqk, "bqk": bqk, "wr": wr, "br": br, "lng": lng,
              "lnb": lnb, "w1": w1, "b1": b1, "w2": w2, "b2": b2, "idf": idf,
              "cst": cst}
    in_maps = [dict(shared, x=np.ascontiguousarray(x[b])) for b in range(8)]
    res = run_bass_kernel_spmd(nc, in_maps, core_ids=list(range(8)))
    out = np.stack([np.asarray(res.results[b]["out"]) for b in range(8)], axis=0)
    return out.astype(np.float32)
